# version 19 (final)
# speedup vs baseline: 1.3085x; 1.0156x over previous
import numpy as np
import ml_dtypes
from contextlib import ExitStack
import concourse.bass as bass
import concourse.mybir as mybir
from concourse.bass_utils import run_bass_kernel_spmd

BF16 = mybir.dt.bfloat16
F32 = mybir.dt.float32
I8 = mybir.dt.int8
AF = mybir.ActivationFunctionType
ALU = mybir.AluOpType
AX = mybir.AxisListType
bfdt = ml_dtypes.bfloat16

T = 2048
D = 1024
NE = 8
NQT = 4
NTC = 16
FFH = 3072
NHC = 24
EPS_B = 1e-6
LN_EPS = 1e-5

ENGS = ("sy", "gp", "te", "act", "dve")
PHASES = ("ln1", "attn_a", "ln2", "ffn1", "ln3", "attn_b", "ln4", "ffn2")


class Prog:
    def __init__(self):
        self.ops = {e: [] for e in ENGS}
        self.cnt = {}
        self.wm = {}

    def do(self, eng, fn, sem=None, amt=1):
        if sem is None:
            sem, amt = eng, 1
        if eng == "gp" and sem == "gp":
            prev = self.cnt.get("gp", 0)
            if prev > 0:
                self.wait("gp", ("gp", prev))
        n = self.cnt.get(sem, 0) + amt
        self.cnt[sem] = n

        def op(e, S, fn=fn, sem=sem, amt=amt):
            fn(e).then_inc(S[sem], amt)

        self.ops[eng].append(op)
        return (sem, n)

    def run(self, eng, fn):
        self.ops[eng].append(lambda e, S, fn=fn: fn(e))

    def wait(self, eng, ev):
        if ev is None:
            return
        sem, n = ev
        if self.wm.get((eng, sem), 0) >= n:
            return
        self.wm[(eng, sem)] = n
        self.ops[eng].append(lambda e, S, sem=sem, n=n: e.wait_ge(S[sem], n))

    def waits(self, eng, evs):
        best = {}
        for ev in evs:
            if ev is None:
                continue
            s, n = ev
            if best.get(s, -1) < n:
                best[s] = n
        for s, n in best.items():
            self.wait(eng, (s, n))

    def barrier(self, engs=("te", "act", "dve", "gp")):
        evs = []
        for e in engs:
            n = self.cnt.get(e, 0)
            if n > 0:
                self.wait(e, (e, n))
            evs.append(self.do(e, lambda en: en.nop()))
        for e in engs:
            self.waits(e, evs)
        return evs


class Bank:
    __slots__ = ("t", "off", "free_ev", "data_ev")

    def __init__(self, t, off=0):
        self.t = t
        self.off = off
        self.free_ev = None
        self.data_ev = None

    def ap(self):
        return self.t.ap()[:, self.off:self.off + 512]


def host_consts():
    cones = np.ascontiguousarray(np.tril(np.ones((128, 128), np.float32)).T)
    idsh = np.zeros((64, 128), dtype=np.float32)
    for k in range(64):
        idsh[k, 64 + k] = 1.0
    idf32 = np.eye(128, dtype=np.float32)
    epssel = np.zeros((16, 16 * 65), dtype=np.float32)
    for h in range(16):
        epssel[h, 65 * h + 64] = EPS_B
    return cones.astype(bfdt), idsh.astype(bfdt), idf32, epssel.astype(bfdt)


def build(upto=len(PHASES), dump_xn=False, phase_list=None):
    nc = bass.Bass("TRN2", target_bir_lowering=False)

    xt = nc.dram_tensor("xt", [D, T], F32, kind="ExternalInput").ap()
    w_ip = nc.dram_tensor("w_ip", [D, 3 * D], F32, kind="ExternalInput").ap()
    w_op = nc.dram_tensor("w_op", [D, D], F32, kind="ExternalInput").ap()
    wq = nc.dram_tensor("wq", [D, D], F32, kind="ExternalInput").ap()
    wk = nc.dram_tensor("wk", [D, D], F32, kind="ExternalInput").ap()
    wv = nc.dram_tensor("wv", [D, D], F32, kind="ExternalInput").ap()
    wo = nc.dram_tensor("wo", [D, D], F32, kind="ExternalInput").ap()
    wa = nc.dram_tensor("wa", [D, 16], F32, kind="ExternalInput").ap()
    ba = nc.dram_tensor("ba", [16, 1], F32, kind="ExternalInput").ap()
    wg8 = nc.dram_tensor("wg8", [128, NHC * NE * 128], I8, kind="ExternalInput").ap()
    wu8 = nc.dram_tensor("wu8", [128, NHC * NE * 128], I8, kind="ExternalInput").ap()
    wd8 = nc.dram_tensor("wd8", [128, NE * NHC * 128], I8, kind="ExternalInput").ap()
    c_cones = nc.dram_tensor("c_cones", [128, 128], BF16, kind="ExternalInput").ap()
    c_idsh = nc.dram_tensor("c_idsh", [64, 128], BF16, kind="ExternalInput").ap()
    c_idf = nc.dram_tensor("c_idf", [128, 128], F32, kind="ExternalInput").ap()
    c_eps = nc.dram_tensor("c_eps", [16, 16 * 65], BF16, kind="ExternalInput").ap()
    y = nc.dram_tensor("y", [D, T], F32, kind="ExternalOutput").ap()
    ydbg = nc.dram_tensor("ydbg", [D, T], F32, kind="ExternalOutput").ap() if dump_xn else None

    ctx = ExitStack()
    sb = lambda name, shape, dt: ctx.enter_context(nc.sbuf_tensor(name, shape, dt))
    RS = [sb(f"rs{c}", [128, T], F32) for c in range(NE)]
    XN = [sb(f"xn{c}", [128, T], BF16) for c in range(NE)]
    ARENA = sb("arena", [128, 24704], BF16)
    WREG = [sb(f"wreg{i}", [128, 4096], BF16) for i in range(2)]
    XSQ = sb("xsq", [128, 1024], BF16)
    PPB = sb("ppb", [128, 4096], BF16)
    NUMS = sb("nums", [128, 1024], BF16)
    ATTNH = sb("attnh", [128, 512], BF16)
    GSC = sb("gsc", [128, 1024], BF16)
    ROWA = sb("rowa", [1, 512], F32)
    ISBR = sb("isbr", [1, 512], BF16)
    MSBR = sb("msbr", [1, 512], BF16)
    LND = sb("lnd", [128, 512], F32)
    RCB = sb("rcb", [128, 512], BF16)
    CROW = sb("crow", [1, 64], F32)
    CONES = sb("cones", [128, 128], BF16)
    IDSH = sb("idsh", [64, 128], BF16)
    IDF = sb("idf", [128, 128], F32)
    ONESB = sb("onesb", [128, 512], BF16)
    ONESF = sb("onesf", [128, 128], F32)
    EPSSEL = sb("epssel", [16, 16 * 65], BF16)
    CB = sb("cb", [128, 8], F32)
    LROW = sb("lrow", [16, 1024], F32)
    LTC = sb("ltc", [128, 16], BF16)
    LTS = sb("lts", [128, 256], F32)
    EXPMLS = sb("expmls", [128, 256], F32)
    EXPLCM = sb("explcm", [128, 256], F32)
    EXPLC = sb("explc", [128, 256], F32)
    EXPMLT = sb("expmlt", [16, 2048], BF16)
    ELCR = sb("elcr", [1, 256], BF16)
    LCF = sb("lcf", [1, 256], F32)
    SCR256 = sb("scr256", [128, 256], F32)
    BA2 = sb("ba2", [16, 1], F32)
    T1F = sb("t1f", [128, 1024], F32)
    LNEPS1 = sb("lneps1", [1, 1], F32)
    BASB = sb("basb", [16, 1], F32)
    SSTT = sb("sstt", [128, 260], BF16)
    KTIL = sb("ktil", [128, 256], BF16)
    WSC = sb("wsc", [128, 128], BF16)

    PSD = [ctx.enter_context(nc.psum_tensor(f"psd{i}", [128, 1024], F32)) for i in range(4)]
    B = [Bank(PSD[i // 2], 512 * (i % 2)) for i in range(8)]

    SEMN = ("te", "act", "dve", "gp", "sy", "dx0", "dx1", "dx2", "dx3", "dx4", "dx5",
            "dx6", "dx7", "dcon", "dw0", "dw1", "dw2", "dw3", "dwa", "dy")
    semh = {s: ctx.enter_context(nc.semaphore(s)) for s in SEMN}

    P = Prog()
    A = ARENA
    rs_ev = {}
    xn_guard = [None]

    def AR(off, n):
        return A[:, off:off + n]

    fr = {"wreg0": None, "wreg1": None, "arena": None,
          "w0lo": None, "w0hi": None, "w1lo": None, "w1hi": None,
          "xbf0": None, "xbf1": None, "xsq0": None, "xsq1": None,
          "ppb0": None, "ppb1": None, "ppb2": None, "ppb3": None,
          "ppbp0": None, "ppbp1": None, "ppbp2": None, "ppbp3": None,
          "nums0": None, "nums1": None,
          "attnh": None, "lnd": None, "rcb": None, "gsc0": None, "gsc1": None,
          "ktil0": None, "ktil1": None, "ktil2": None, "ktil3": None,
          "lrow0": None, "lrow1": None}

    ev_xc = []
    for c in range(NE):
        ev = P.do("sy", lambda e, c=c: e.dma_start(out=RS[c][:], in_=xt[128 * c:128 * (c + 1), :]),
                  sem=f"dx{c}", amt=16)
        ev_xc.append(ev)
    ev_x = ev_xc[-1]
    for dst, src in ((CONES, c_cones), (IDSH, c_idsh), (IDF, c_idf), (EPSSEL, c_eps)):
        P.do("sy", lambda e, d=dst, s=src: e.dma_start(out=d[:], in_=s), sem="dcon", amt=16)
    P.do("sy", lambda e: e.dma_start(out=BASB[:], in_=ba), sem="dcon", amt=16)
    ev_consts = ("dcon", P.cnt["dcon"])
    P.wait("dve", ev_consts)
    ev_ones = P.do("dve", lambda e: e.memset(ONESB[:], 1.0))
    ev_onesf = P.do("dve", lambda e: e.memset(ONESF[:], 1.0))
    ev_ba2 = P.do("dve", lambda e: e.tensor_scalar(out=BA2[:], in0=BASB[:], scalar1=2.0,
                                                   scalar2=None, op0=ALU.add))
    ev_lneps = P.do("dve", lambda e: e.memset(LNEPS1[:], LN_EPS))

    def te_last():
        return ("te", P.cnt["te"])

    def emit_ln():
        xn_evs = []
        for qt in range(NQT):
            cols = slice(512 * qt, 512 * (qt + 1))
            sb0, sb1 = B[2 * (qt % 2)], B[2 * (qt % 2) + 1]
            for c in range(NE):
                slot = c % 2
                xq = XSQ[:, 512 * slot:512 * slot + 512]
                P.waits("act", [fr[f"xsq{slot}"], ev_xc[c], rs_ev.get((c, qt))])
                ev_sq = P.do("act", lambda e, xq=xq, c=c, cols=cols:
                             e.activation(out=xq, in_=RS[c][:, cols], func=AF.Square))
                if c == 0:
                    for b in (sb0, sb1):
                        if b.free_ev is not None:
                            P.waits("te", b.free_ev if isinstance(b.free_ev, list) else [b.free_ev])
                P.waits("te", [ev_sq, ev_ones, ev_onesf, rs_ev.get((c, qt))])
                evm1 = P.do("te", lambda e, b=sb0, c=c, cols=cols, s=(c == 0), t=(c == NE - 1):
                            e.matmul(b.ap()[0:1, :], ONESF[:, 0:1], RS[c][:, cols],
                                     start=s, stop=t))
                evm2 = P.do("te", lambda e, b=sb1, xq=xq, s=(c == 0), t=(c == NE - 1):
                            e.matmul(b.ap()[0:1, :], ONESB[:, 0:1], xq, start=s, stop=t))
                fr[f"xsq{slot}"] = evm2
            sb0.data_ev, sb1.data_ev = evm1, evm2
            P.waits("act", [sb0.data_ev, fr.get("rowa")])
            ev_m2 = P.do("act", lambda e, b=sb0: e.activation(out=ROWA[:], in_=b.ap()[0:1, :],
                                                              func=AF.Square, scale=1.0 / D))
            P.waits("dve", [ev_m2, sb1.data_ev, fr["lnd"]])
            ev_var = P.do("dve", lambda e, b=sb1: e.scalar_tensor_tensor(
                out=LND[0:1, :], in0=b.ap()[0:1, :], scalar=1.0 / D, in1=ROWA[:],
                op0=ALU.mult, op1=ALU.subtract))
            P.waits("act", [ev_var, ev_lneps])
            ev_ln = P.do("act", lambda e: e.activation(out=ROWA[:], in_=LND[0:1, :],
                                                       func=AF.Ln, bias=LNEPS1[:]))
            P.wait("act", ev_ln)
            ev_is = P.do("act", lambda e: e.activation(out=ISBR[:], in_=ROWA[:],
                                                       func=AF.Exp, scale=-0.5))
            fr["rowa"] = ev_is
            P.waits("dve", [ev_is, sb0.data_ev])
            ev_ms = P.do("dve", lambda e, b=sb0: e.tensor_scalar(
                out=MSBR[:], in0=b.ap()[0:1, :], scalar1=1.0 / D, scalar2=None,
                op0=ALU.mult))
            sb0.free_ev, sb1.free_ev = [ev_ms, ev_m2], ev_var
            for b in (B[4], B[5]):
                if b.free_ev is not None:
                    P.waits("te", b.free_ev if isinstance(b.free_ev, list) else [b.free_ev])
            P.waits("te", [ev_is, ev_ms])
            ev_bis = P.do("te", lambda e: e.matmul(B[4].ap()[:, :], ONESB[0:1, 0:128],
                                                   ISBR[:], start=True, stop=True))
            ev_bms = P.do("te", lambda e: e.matmul(B[5].ap()[:, :], ONESB[0:1, 0:128],
                                                   MSBR[:], start=True, stop=True))
            last = None
            prev = {0: None, 1: None}
            for c in range(NE):
                sl = c % 2
                t1 = T1F[:, 512 * sl:512 * sl + 512]
                P.waits("dve", [ev_bis, ev_bms, prev[sl], xn_guard[0]])
                ev_t1 = P.do("dve", lambda e, c=c, cols=cols, t1=t1: e.tensor_tensor(
                    out=t1, in0=RS[c][:, cols], in1=B[5].ap()[:, :], op=ALU.subtract))
                P.wait("dve", ev_t1)
                last = P.do("dve", lambda e, c=c, cols=cols, t1=t1: e.tensor_tensor(
                    out=XN[c][:, cols], in0=t1, in1=B[4].ap()[:, :], op=ALU.mult))
                prev[sl] = last
            B[4].free_ev = B[5].free_ev = last
            xn_evs.append(last)
        return xn_evs

    def emit_attn_a():
        QAp = [AR(0, 4096), AR(8192, 4096)]
        KAp = [AR(4096, 4096), AR(12288, 4096)]
        VAa = AR(16384, 4160)
        AOUTa = AR(20544, 4096)
        va4 = VAa.rearrange("p (h c x) -> p h c x", h=4, c=16)
        arena_qk_free = [None, None]
        arena_v_free = [None]
        proj_state = {}
        state = {"rs_last": None}

        def wload(src3, wbuf, lo, sem, ncols=2048, k=8):
            key = f"w{wbuf}{'lo' if lo else 'hi'}"
            P.wait("gp", fr[key])
            off = 0 if lo else 2048
            dst3 = WREG[wbuf][:, off:off + ncols].rearrange("p (k j) -> p k j", k=k)
            return P.do("gp", lambda e, s=src3, d=dst3: e.dma_start(out=d, in_=s),
                        sem=sem, amt=16)

        def make_qk_units(qn):
            pi = qn % 2
            f0 = 256 * qn
            srcq = w_ip[:, f0:f0 + 256].rearrange("(ec p) j -> p ec j", p=128)
            srck = w_ip[:, 1024 + f0:1024 + f0 + 256].rearrange("(ec p) j -> p ec j", p=128)
            evq = wload(srcq, 0, True, "dw0")
            evk = wload(srck, 1, True, "dw1")
            ps = proj_state[qn] = {"q_mm": None, "k_mm": None, "evac": None}
            units = []
            for which in range(2):
                for pt2 in range(2):
                    for qt2 in range(4):
                        def half1(which=which, pt2=pt2, qt2=qt2, pi=pi,
                                  evl=(evq if which == 0 else evk)):
                            bk = B[6]
                            if bk.free_ev is not None:
                                P.waits("te", bk.free_ev if isinstance(bk.free_ev, list) else [bk.free_ev])
                            P.waits("te", [evl, ev_xn_ready[qt2], arena_qk_free[pi]])
                            for ec in range(4):
                                lhsT = WREG[which][:, 256 * ec + 128 * pt2:256 * ec + 128 * pt2 + 128]
                                rhs = XN[ec][:, 512 * qt2:512 * qt2 + 512]
                                P.do("te", lambda e, b=bk, l=lhsT, r=rhs, s=(ec == 0):
                                     e.matmul(b.ap()[:, :], l, r, start=s, stop=False))

                        def half2(which=which, pt2=pt2, qt2=qt2, pi=pi, ps=ps):
                            bk = B[6]
                            for ec in range(4, NE):
                                lhsT = WREG[which][:, 256 * ec + 128 * pt2:256 * ec + 128 * pt2 + 128]
                                rhs = XN[ec][:, 512 * qt2:512 * qt2 + 512]
                                P.do("te", lambda e, b=bk, l=lhsT, r=rhs, t=(ec == NE - 1):
                                     e.matmul(b.ap()[:, :], l, r, start=False, stop=t))
                            bk.data_ev = te_last()
                            if which == 0:
                                ps["q_mm"] = bk.data_ev
                            else:
                                ps["k_mm"] = bk.data_ev
                            dstbuf = QAp[pi] if which == 0 else KAp[pi]
                            dst = dstbuf[:, 2048 * pt2 + 512 * qt2:2048 * pt2 + 512 * qt2 + 512]
                            P.waits("dve", [bk.data_ev, arena_qk_free[pi]])
                            if which == 0:
                                ev = P.do("dve", lambda e, d=dst, b=bk: e.tensor_scalar(
                                    out=d, in0=b.ap()[:, :], scalar1=0.125, scalar2=None,
                                    op0=ALU.mult))
                            else:
                                ev = P.do("dve", lambda e, d=dst, b=bk: e.tensor_copy(
                                    out=d, in_=b.ap()[:, :]))
                            bk.free_ev = ev
                            ps["evac"] = ev
                        units.append(half1)
                        units.append(half2)
            return units

        def emit_cbound(qr):
            pi = qr % 2
            ev_evac = proj_state[qr]["evac"]
            for i, buf in enumerate((QAp[pi], KAp[pi])):
                for pt in range(2):
                    for hh in range(2):
                        xq = XSQ[:, 0:1024]
                        sq = buf[:, 2048 * pt + 1024 * hh:2048 * pt + 1024 * hh + 1024]
                        P.waits("act", [ev_evac, fr["xsq0"]])
                        e1 = P.do("act", lambda e, d=xq, s=sq: e.activation(
                            out=d, in_=s, func=AF.Square))
                        for sub in range(2):
                            bk = B[4]
                            if bk.free_ev is not None:
                                P.waits("te", bk.free_ev if isinstance(bk.free_ev, list) else [bk.free_ev])
                            P.waits("te", [e1, ev_ones])
                            evm = P.do("te", lambda e, r=xq[:, 512 * sub:512 * sub + 512]:
                                       e.matmul(B[4].ap()[0:1, :], ONESB[:, 0:1], r,
                                                start=True, stop=True))
                            slot = i * 8 + pt * 4 + hh * 2 + sub
                            P.wait("dve", evm)
                            evr = P.do("dve", lambda e, sl=slot: e.reduce_max(
                                out=CROW[0:1, sl:sl + 1], in_=B[4].ap()[0:1, :], axis=AX.X))
                            bk.free_ev = evr
                        fr["xsq0"] = te_last()
            P.wait("dve", ("dve", P.cnt["dve"]))
            P.do("dve", lambda e: e.reduce_max(out=CROW[0:1, 32:33], in_=CROW[0:1, 0:8], axis=AX.X))
            P.do("dve", lambda e: e.reduce_max(out=CROW[0:1, 33:34], in_=CROW[0:1, 8:16], axis=AX.X))
            P.wait("dve", ("dve", P.cnt["dve"]))
            ev_pr = P.do("dve", lambda e: e.tensor_tensor(out=CROW[0:1, 34:35], in0=CROW[0:1, 32:33],
                                                          in1=CROW[0:1, 33:34], op=ALU.mult))
            P.wait("act", ev_pr)
            ev_c = P.do("act", lambda e: e.activation(out=CROW[0:1, 35:36], in_=CROW[0:1, 34:35],
                                                      func=AF.Sqrt))
            P.waits("dve", [ev_c, fr["rcb"]])
            ev_cn = P.do("dve", lambda e: e.tensor_scalar(out=RCB[0:1, 0:1], in0=CROW[0:1, 35:36],
                                                          scalar1=-1.0, scalar2=None, op0=ALU.mult))
            if B[5].free_ev is not None:
                P.waits("te", B[5].free_ev if isinstance(B[5].free_ev, list) else [B[5].free_ev])
            P.waits("te", [ev_cn, ev_ones])
            ev_cb = P.do("te", lambda e: e.matmul(B[5].ap()[:, 0:1], ONESB[0:1, 0:128],
                                                  RCB[0:1, 0:1], start=True, stop=True))
            P.wait("dve", ev_cb)
            ev_cbs = P.do("dve", lambda e, qr=qr: e.tensor_copy(out=CB[:, qr:qr + 1],
                                                                in_=B[5].ap()[:, 0:1]))
            B[5].free_ev = ev_cbs
            fr["rcb"] = ev_cb
            return ev_cbs

        def emit_attention(qr, ev_cbs, ev_vev, fillers):
            pi = qr % 2
            QA, KA = QAp[pi], KAp[pi]
            ev_evac = proj_state[qr]["evac"]
            ev_aout = []
            fidx = [0]
            nstep = [0]

            def tick():
                nstep[0] += 1
                if nstep[0] % 2 == 0 and fidx[0] < len(fillers):
                    fillers[fidx[0]]()
                    fidx[0] += 1

            for hp in range(2):
                pt = hp
                for qt in range(NQT):
                    obs = [B[4], B[5]]
                    for ob in obs:
                        if ob.free_ev is not None:
                            P.waits("te", ob.free_ev if isinstance(ob.free_ev, list) else [ob.free_ev])
                    kcmax = 4 * qt + 4
                    LAG = 2
                    ev_pv = [None, None]
                    ev_exp = {}
                    P.waits("te", [ev_evac])

                    def front_a(kc):
                        pair = kc % 2
                        pslot = kc % 4
                        ev_ss = None
                        for uu in range(2):
                            sbk = B[2 * pair + uu]
                            if sbk.free_ev is not None:
                                P.waits("te", sbk.free_ev if isinstance(sbk.free_ev, list) else [sbk.free_ev])
                            base = 64 * uu
                            lhsT = KA[base:base + 64, 2048 * pt + 128 * kc:2048 * pt + 128 * kc + 128]
                            rhs = QA[base:base + 64, 2048 * pt + 512 * qt:2048 * pt + 512 * qt + 512]
                            ev_ss = P.do("te", lambda e, b=sbk, l=lhsT, r=rhs:
                                         e.matmul(b.ap()[:, :], l, r, start=True, stop=True))
                        ppair = PPB[:, 1024 * pslot:1024 * pslot + 1024]
                        P.waits("act", [ev_ss, ev_cbs, fr[f"ppbp{pslot}"]])
                        ev_pp2 = P.do("act", lambda e, d=ppair, pr=pair, qr=qr: e.activation(
                            out=d, in_=PSD[pr].ap()[:, :], func=AF.Exp, bias=CB[:, qr:qr + 1]))
                        B[2 * pair].free_ev = ev_pp2
                        B[2 * pair + 1].free_ev = ev_pp2
                        for uu in range(2):
                            ev_pp = ev_pp2
                            pslice = PPB[:, 1024 * pslot + 512 * uu:1024 * pslot + 512 * uu + 512]
                            if kc >= 4 * qt:
                                crel = kc - 4 * qt
                                P.waits("dve", [ev_pp, ev_consts])
                                if crel > 0:
                                    P.do("dve", lambda e, d=pslice, cr=crel:
                                         e.memset(d[:, 0:128 * cr], 0.0))
                                ev_pp = P.do("dve", lambda e, d=pslice, cr=crel: e.tensor_tensor(
                                    out=d[:, 128 * cr:128 * cr + 128],
                                    in0=d[:, 128 * cr:128 * cr + 128], in1=CONES[:], op=ALU.mult))
                            ev_exp[(kc, uu)] = (ev_pp, pslice)

                    def back_a(kc, u):
                        h = 2 * hp + u
                        ob = obs[u]
                        ev_pp, pslice = ev_exp.pop((kc, u))
                        vblk = VAa[:, 65 * (h * 16 + kc):65 * (h * 16 + kc) + 65]
                        P.waits("te", [ev_pp, ev_vev[kc]])
                        ev_pv[u] = P.do("te", lambda e, b=ob, v=vblk, p=pslice,
                                        s=(kc == 0), t=(kc == kcmax - 1):
                                        e.matmul(b.ap()[0:65, :], v, p, start=s, stop=t))
                        if u == 1:
                            fr[f"ppbp{kc % 4}"] = ev_pv[u]

                    for i in range(kcmax + LAG):
                        if i < kcmax:
                            front_a(i)
                        if i >= LAG:
                            back_a(i - LAG, 0)
                            back_a(i - LAG, 1)
                        tick()
                    epi = {}
                    for u in range(2):
                        ob = obs[u]
                        ob.data_ev = ev_pv[u]
                        nsl = u
                        P.waits("dve", [fr[f"nums{nsl}"], ob.data_ev])
                        ev_nm = P.do("dve", lambda e, b=ob, n=nsl: e.tensor_copy(
                            out=NUMS[0:64, 512 * n:512 * n + 512], in_=b.ap()[0:64, :]))
                        epi[u] = ev_nm
                    for u in range(2):
                        h = 2 * hp + u
                        ob = obs[u]
                        nsl = u
                        ev_nm = epi[u]
                        P.waits("act", [ob.data_ev, fr["lnd"], fr["rcb"]])
                        ev_l = P.do("act", lambda e, b=ob: e.activation(
                            out=LND[64:65, :], in_=b.ap()[64:65, :], func=AF.Ln))
                        P.wait("act", ev_l)
                        ev_rc = P.do("act", lambda e: e.activation(
                            out=RCB[64:65, :], in_=LND[64:65, :], func=AF.Exp, scale=-1.0))
                        fr["lnd"] = ev_rc
                        rbk = B[7]
                        if rbk.free_ev is not None:
                            P.waits("te", rbk.free_ev if isinstance(rbk.free_ev, list) else [rbk.free_ev])
                        P.waits("te", [ev_rc])
                        ev_rb = P.do("te", lambda e, b=rbk: e.matmul(
                            b.ap()[0:64, :], ONESB[64:65, 0:64], RCB[64:65, :], start=True, stop=True))
                        fr["rcb"] = ev_rb
                        if u == 0:
                            dst = AOUTa[0:64, 2048 * pt + 512 * qt:2048 * pt + 512 * qt + 512]
                            P.waits("dve", [ev_nm, ev_rb])
                            ev_mul = P.do("dve", lambda e, d=dst, b=rbk, n=nsl: e.tensor_tensor(
                                out=d, in0=NUMS[0:64, 512 * n:512 * n + 512],
                                in1=b.ap()[0:64, :], op=ALU.mult))
                            rbk.free_ev = ev_mul
                            ob.free_ev = ev_mul
                            fr[f"nums{nsl}"] = ev_mul
                            ev_aout.append(ev_mul)
                        else:
                            P.waits("dve", [ev_nm, ev_rb, fr["attnh"]])
                            ev_mul = P.do("dve", lambda e, b=rbk, n=nsl: e.tensor_tensor(
                                out=ATTNH[0:64, :], in0=NUMS[0:64, 512 * n:512 * n + 512],
                                in1=b.ap()[0:64, :], op=ALU.mult))
                            rbk.free_ev = ev_mul
                            ob.free_ev = ev_mul
                            fr[f"nums{nsl}"] = ev_mul
                            mbk = B[7]
                            P.waits("te", [ev_mul, ev_consts])
                            ev_mv = P.do("te", lambda e, b=mbk: e.matmul(
                                b.ap()[:, :], IDSH[:], ATTNH[0:64, :], start=True, stop=True))
                            fr["attnh"] = ev_mv
                            dst = AOUTa[64:128, 2048 * pt + 512 * qt:2048 * pt + 512 * qt + 512]
                            P.wait("dve", ev_mv)
                            ev_mc = P.do("dve", lambda e, d=dst, b=mbk: e.tensor_copy(
                                out=d, in_=b.ap()[64:128, :]))
                            mbk.free_ev = ev_mc
                            ev_aout.append(ev_mc)
            while fidx[0] < len(fillers):
                fillers[fidx[0]]()
                fidx[0] += 1
            return ev_aout

        P.wait("dve", fr["arena"])
        ev_vones = P.do("dve", lambda e: e.memset(va4[:, :, :, 64:65], 1.0))

        units0 = make_qk_units(0)
        for u in units0:
            u()
        fr["w0lo"] = proj_state[0]["q_mm"]
        fr["w1lo"] = proj_state[0]["k_mm"]

        for qr in range(4):
            pi = qr % 2
            f0 = 256 * qr
            srcv = w_ip[:, 2048 + f0:2048 + f0 + 256].rearrange("(ec p) j -> p ec j", p=128)
            srco = w_op[f0:f0 + 256, :].rearrange("(k p) j -> p k j", p=128)
            evv = wload(srcv, 1, False, "dw3")
            evo = wload(srco, 0, False, "dw2", k=2)
            ev_vev = []
            for tc in range(NTC):
                bk = B[2 + tc % 2]
                if bk.free_ev is not None:
                    P.waits("te", bk.free_ev if isinstance(bk.free_ev, list) else [bk.free_ev])
                P.waits("te", [evv, ev_xn_ready[tc // 4], arena_v_free[0]])
                for ec in range(NE):
                    lhsT = XN[ec][:, 128 * tc:128 * tc + 128]
                    rhs = WREG[1][:, 2048 + 256 * ec:2048 + 256 * ec + 256]
                    P.do("te", lambda e, b=bk, l=lhsT, r=rhs, s=(ec == 0), t=(ec == NE - 1):
                         e.matmul(b.ap()[:, 0:256], l, r, start=s, stop=t))
                bk.data_ev = te_last()
                dsel = va4[:, :, tc, 0:64]
                srcap = bk.ap()[:, 0:256].rearrange("p (h x) -> p h x", h=4)
                P.waits("dve", [bk.data_ev, ev_vones, arena_v_free[0]])
                ev = P.do("dve", lambda e, d=dsel, s=srcap: e.tensor_copy(out=d, in_=s))
                bk.free_ev = ev
                ev_vev.append(ev)
            fr["w1hi"] = te_last()
            ev_cbs = emit_cbound(qr)
            fillers = make_qk_units(qr + 1) if qr < 3 else []
            ev_aout = emit_attention(qr, ev_cbs, ev_vev, fillers)
            arena_qk_free[pi] = te_last()
            arena_v_free[0] = te_last()
            if qr < 3:
                fr["w0lo"] = proj_state[qr + 1]["q_mm"]
                fr["w1lo"] = proj_state[qr + 1]["k_mm"]
            for fc in range(NE):
                for qt in range(NQT):
                    bk = B[6 + (fc * 4 + qt) % 2]
                    if bk.free_ev is not None:
                        P.waits("te", bk.free_ev if isinstance(bk.free_ev, list) else [bk.free_ev])
                    P.waits("te", [evo] + ev_aout)
                    for pt in range(2):
                        lhsT = WREG[0][:, 2048 + 1024 * pt + 128 * fc:2048 + 1024 * pt + 128 * fc + 128]
                        rhs = AOUTa[:, 2048 * pt + 512 * qt:2048 * pt + 512 * qt + 512]
                        P.do("te", lambda e, b=bk, l=lhsT, r=rhs, s=(pt == 0), t=(pt == 1):
                             e.matmul(b.ap()[:, :], l, r, start=s, stop=t))
                    bk.data_ev = te_last()
                    P.wait("dve", bk.data_ev)
                    ev = P.do("dve", lambda e, fc=fc, qt=qt, b=bk: e.scalar_tensor_tensor(
                        out=RS[fc][:, 512 * qt:512 * qt + 512], in0=b.ap()[:, :], scalar=1.0,
                        in1=RS[fc][:, 512 * qt:512 * qt + 512], op0=ALU.mult, op1=ALU.add))
                    bk.free_ev = ev
                    rs_ev[(fc, qt)] = ev
                    state["rs_last"] = ev
            fr["w0hi"] = te_last()
        fr["arena"] = state["rs_last"]
        xn_guard[0] = te_last()
        return state["rs_last"]

    def emit_ffn():
        HID = AR(0, 24576)
        ev_rs_last = None
        for ts in range(2):
            ev_hid = []
            for hc in range(NHC):
                wb = hc % 2
                P.waits("gp", [fr[f"w{wb}lo"], fr[f"w{wb}hi"]])
                evg = P.do("gp", lambda e, hc=hc, wb=wb: e.dma_start(
                    out=WREG[wb][:, 0:1024], in_=wg8[:, 1024 * hc:1024 * hc + 1024]),
                    sem=f"dw{wb}", amt=16)
                evu = P.do("gp", lambda e, hc=hc, wb=wb: e.dma_start(
                    out=WREG[wb][:, 1024:2048], in_=wu8[:, 1024 * hc:1024 * hc + 1024]),
                    sem=f"dw{wb}", amt=16)
                for tq in range(2):
                    gb, ub = B[tq % 2], B[2 + tq % 2]
                    for b in (gb, ub):
                        if b.free_ev is not None:
                            P.waits("te", b.free_ev if isinstance(b.free_ev, list) else [b.free_ev])
                    P.waits("te", [evg, evu, ev_xn_ready[2 * ts + tq]])
                    cols = slice(1024 * ts + 512 * tq, 1024 * ts + 512 * tq + 512)
                    for ec in range(NE):
                        lhsT = WREG[wb][:, 128 * ec:128 * ec + 128]
                        rhs = XN[ec][:, cols]
                        P.do("te", lambda e, b=gb, l=lhsT, r=rhs, s=(ec == 0), t=(ec == NE - 1):
                             e.matmul(b.ap()[:, :], l, r, start=s, stop=t))
                    gb.data_ev = te_last()
                    for ec in range(NE):
                        lhsT = WREG[wb][:, 1024 + 128 * ec:1024 + 128 * ec + 128]
                        rhs = XN[ec][:, cols]
                        P.do("te", lambda e, b=ub, l=lhsT, r=rhs, s=(ec == 0), t=(ec == NE - 1):
                             e.matmul(b.ap()[:, :], l, r, start=s, stop=t))
                    ub.data_ev = te_last()
                    gs = GSC[:, 512 * tq:512 * tq + 512]
                    P.waits("act", [gb.data_ev, fr[f"gsc{tq}"]])
                    ev_si = P.do("act", lambda e, d=gs, b=gb: e.activation(
                        out=d, in_=b.ap()[:, :], func=AF.Silu, scale=1.0 / 127.0))
                    gb.free_ev = ev_si
                    dst = HID[:, 1024 * hc + 512 * tq:1024 * hc + 512 * tq + 512]
                    P.waits("dve", [ev_si, ub.data_ev, fr["arena"]])
                    ev_h = P.do("dve", lambda e, d=dst, b=ub, g=gs: e.scalar_tensor_tensor(
                        out=d, in0=b.ap()[:, :], scalar=1.0 / 127.0, in1=g,
                        op0=ALU.mult, op1=ALU.mult))
                    ub.free_ev = ev_h
                    fr[f"gsc{tq}"] = ev_h
                    ev_hid.append(ev_h)
                fr[f"w{wb}lo"] = te_last()
                fr[f"w{wb}hi"] = te_last()
            for dc in range(NE):
                wb = dc % 2
                P.waits("gp", [fr[f"w{wb}lo"], fr[f"w{wb}hi"]])
                evd = P.do("gp", lambda e, dc=dc, wb=wb: e.dma_start(
                    out=WREG[wb][:, 0:3072], in_=wd8[:, 3072 * dc:3072 * dc + 3072],
                    max_dma_last_dim=2048), sem=f"dw{wb}", amt=16)
                for tq in range(2):
                    bk = B[4 + tq % 2]
                    if bk.free_ev is not None:
                        P.waits("te", bk.free_ev if isinstance(bk.free_ev, list) else [bk.free_ev])
                    P.waits("te", [evd] + ev_hid)
                    for hc in range(NHC):
                        lhsT = WREG[wb][:, 128 * hc:128 * hc + 128]
                        rhs = HID[:, 1024 * hc + 512 * tq:1024 * hc + 512 * tq + 512]
                        P.do("te", lambda e, b=bk, l=lhsT, r=rhs, s=(hc == 0), t=(hc == NHC - 1):
                             e.matmul(b.ap()[:, :], l, r, start=s, stop=t))
                    bk.data_ev = te_last()
                    cols = slice(1024 * ts + 512 * tq, 1024 * ts + 512 * tq + 512)
                    P.wait("dve", bk.data_ev)
                    ev = P.do("dve", lambda e, dc=dc, cols=cols, b=bk: e.scalar_tensor_tensor(
                        out=RS[dc][:, cols], in0=b.ap()[:, :], scalar=1.0 / 127.0,
                        in1=RS[dc][:, cols], op0=ALU.mult, op1=ALU.add))
                    bk.free_ev = ev
                    rs_ev[(dc, 2 * ts + tq)] = ev
                    ev_rs_last = ev
                fr[f"w{wb}lo"] = te_last()
                fr[f"w{wb}hi"] = te_last()
            fr["arena"] = ev_rs_last
        xn_guard[0] = te_last()
        return ev_rs_last

    def emit_attn_b():
        QB, KB = AR(0, 4096), AR(4096, 4096)
        KT, VAb = AR(8192, 4096), AR(12352, 4160)
        AOUTb = AR(16512, 4096)
        vb4 = VAb.rearrange("p (h c x) -> p h c x", h=4, c=16)

        P.wait("gp", ev_x)
        evwa = P.do("gp", lambda e: e.dma_start(
            out=WSC[:, 0:128].rearrange("p (k j) -> p k j", k=8),
            in_=wa.rearrange("(k p) j -> p k j", p=128)), sem="dwa", amt=16)
        for qt in range(NQT):
            bk = B[6]
            if bk.free_ev is not None:
                P.waits("te", bk.free_ev if isinstance(bk.free_ev, list) else [bk.free_ev])
            P.waits("te", [evwa, ev_xn_ready[qt]])
            for ec in range(NE):
                P.do("te", lambda e, b=bk, l=WSC[:, 16 * ec:16 * ec + 16],
                     r=XN[ec][:, 512 * qt:512 * qt + 512], s=(ec == 0), t=(ec == NE - 1):
                     e.matmul(b.ap()[0:16, :], l, r, start=s, stop=t))
            bk.data_ev = te_last()
            lsl = qt % 2
            P.waits("act", [bk.data_ev, ev_ba2, fr[f"lrow{lsl}"]])
            ev_sg = P.do("act", lambda e, b=bk: e.activation(
                out=LND[0:16, :], in_=b.ap()[0:16, :], func=AF.Sigmoid, bias=BA2[:]))
            P.wait("act", ev_sg)
            ev_lr = P.do("act", lambda e, sl=lsl: e.activation(
                out=LROW[:, 512 * sl:512 * sl + 512], in_=LND[0:16, :], func=AF.Ln))
            bk.free_ev = ev_lr
            for cc in range(4):
                c = 4 * qt + cc
                tb = B[7]
                if tb.free_ev is not None:
                    P.waits("te", tb.free_ev if isinstance(tb.free_ev, list) else [tb.free_ev])
                P.waits("te", [ev_lr, ev_consts])
                ev_t1 = P.do("te", lambda e, b=tb, sl=lsl, cc=cc: e.transpose(
                    b.ap()[:, 0:16], LROW[:, 512 * sl + 128 * cc:512 * sl + 128 * cc + 128],
                    IDF[0:16, 0:16]))
                P.wait("dve", ev_t1)
                ev_ltc = P.do("dve", lambda e, b=tb: e.tensor_copy(out=LTC[:], in_=b.ap()[:, 0:16]))
                tb.free_ev = ev_ltc
                cb = B[6]
                if cb.free_ev is not None:
                    P.waits("te", cb.free_ev if isinstance(cb.free_ev, list) else [cb.free_ev])
                P.waits("te", [ev_ltc, ev_consts])
                ev_cs = P.do("te", lambda e, b=cb: e.matmul(
                    b.ap()[:, 0:16], CONES[:], LTC[:], start=True, stop=True))
                ev_lc = P.do("te", lambda e, b=cb: e.matmul(
                    b.ap()[0:1, 16:32], ONESB[:, 0:1], LTC[:], start=True, stop=True))
                P.wait("dve", ev_lc)
                ev_lts = P.do("dve", lambda e, b=cb, c=c: e.tensor_copy(
                    out=LTS[:, 16 * c:16 * c + 16], in_=b.ap()[:, 0:16]))
                ev_lcf = P.do("dve", lambda e, b=cb, c=c: e.tensor_copy(
                    out=LCF[0:1, 16 * c:16 * c + 16], in_=b.ap()[0:1, 16:32]))
                P.wait("act", ev_lc)
                ev_ems = P.do("act", lambda e, b=cb, c=c: e.activation(
                    out=EXPMLS[:, 16 * c:16 * c + 16], in_=b.ap()[:, 0:16],
                    func=AF.Exp, scale=-1.0))
                ev_elc = P.do("act", lambda e, b=cb, c=c: e.activation(
                    out=ELCR[0:1, 16 * c:16 * c + 16], in_=b.ap()[0:1, 16:32], func=AF.Exp))
                cb.free_ev = [ev_lcf, ev_elc]
                bb = B[7]
                if bb.free_ev is not None:
                    P.waits("te", bb.free_ev if isinstance(bb.free_ev, list) else [bb.free_ev])
                P.waits("te", [ev_lts, ev_consts])
                ev_bt = P.do("te", lambda e, b=bb, c=c: e.transpose(
                    b.ap()[0:16, 0:128], LTS[:, 16 * c:16 * c + 16], IDF[:]))
                P.wait("act", ev_bt)
                ev_emt = P.do("act", lambda e, b=bb, c=c: e.activation(
                    out=EXPMLT[:, 128 * c:128 * c + 128], in_=b.ap()[0:16, 0:128],
                    func=AF.Exp, scale=-1.0))
                bb.free_ev = ev_emt
            fr[f"lrow{lsl}"] = te_last()
        ev_decay_t = te_last()
        bk = B[6]
        if bk.free_ev is not None:
            P.waits("te", bk.free_ev if isinstance(bk.free_ev, list) else [bk.free_ev])
        P.waits("te", [ev_onesf])
        ev_blc = P.do("te", lambda e, b=bk: e.matmul(
            b.ap()[:, 0:256], ONESF[0:1, :], LCF[:], start=True, stop=True))
        P.wait("dve", ev_blc)
        ev_sub = P.do("dve", lambda e, b=bk: e.tensor_tensor(
            out=SCR256[:], in0=b.ap()[:, 0:256], in1=LTS[:], op=ALU.subtract))
        bk.free_ev = ev_sub
        P.wait("act", ev_sub)
        ev_elcm = P.do("act", lambda e: e.activation(out=EXPLCM[:], in_=SCR256[:], func=AF.Exp))
        bk2 = B[7]
        if bk2.free_ev is not None:
            P.waits("te", bk2.free_ev if isinstance(bk2.free_ev, list) else [bk2.free_ev])
        ev_belc = P.do("te", lambda e, b=bk2: e.matmul(
            b.ap()[:, 0:256], ONESB[0:1, 0:128], ELCR[:], start=True, stop=True))
        P.wait("dve", ev_belc)
        ev_explc = P.do("dve", lambda e, b=bk2: e.tensor_copy(out=EXPLC[:], in_=b.ap()[:, 0:256]))
        bk2.free_ev = ev_explc

        ev_rs_last = None
        for qr in range(4):
            p0 = 256 * qr
            srcq = wq[:, p0:p0 + 256].rearrange("(ec p) j -> p ec j", p=128)
            srck = wk[:, p0:p0 + 256].rearrange("(ec p) j -> p ec j", p=128)
            srcv = wv[:, p0:p0 + 256].rearrange("(ec p) j -> p ec j", p=128)
            srco = wo[p0:p0 + 256, :].rearrange("(k p) j -> p k j", p=128)

            def load_region(src3, wbuf, ncols, k=8):
                P.wait("gp", fr[f"w{wbuf}lo"])
                dst3 = WREG[wbuf][:, 0:ncols].rearrange("p (k j) -> p k j", k=k)
                return P.do("gp", lambda e, s=src3, d=dst3: e.dma_start(out=d, in_=s),
                            sem=f"dw{wbuf}", amt=16)

            evq = load_region(srcq, 0, 2048)
            evk = load_region(srck, 1, 2048)

            def elu_evac(bk, dst):
                P.waits("act", [bk.data_ev, fr["gsc0"], fr["arena"]])
                e1 = P.do("act", lambda e, b=bk: e.activation(
                    out=GSC[:, 0:512], in_=b.ap()[:, :], func=AF.Exp))
                e2 = P.do("act", lambda e, b=bk: e.activation(
                    out=GSC[:, 512:1024], in_=b.ap()[:, :], func=AF.Relu))
                bk.free_ev = e2
                P.wait("dve", e2)
                ev = P.do("dve", lambda e, d=dst: e.scalar_tensor_tensor(
                    out=d, in0=GSC[:, 0:512], scalar=1.0, in1=GSC[:, 512:1024],
                    op0=ALU.min, op1=ALU.add))
                fr["gsc0"] = ev
                return ev

            ev_qke = []
            for which, wb, ev_wl, dstb in ((0, 0, evq, QB), (1, 1, evk, KB)):
                for pt in range(2):
                    for qt in range(4):
                        bk = B[(pt * 4 + qt) % 2]
                        if bk.free_ev is not None:
                            P.waits("te", bk.free_ev if isinstance(bk.free_ev, list) else [bk.free_ev])
                        P.waits("te", [ev_wl, ev_xn_ready[qt]])
                        for ec in range(NE):
                            lhsT = WREG[wb][:, 256 * ec + 128 * pt:256 * ec + 128 * pt + 128]
                            rhs = XN[ec][:, 512 * qt:512 * qt + 512]
                            P.do("te", lambda e, b=bk, l=lhsT, r=rhs, s=(ec == 0), t=(ec == NE - 1):
                                 e.matmul(b.ap()[:, :], l, r, start=s, stop=t))
                        bk.data_ev = te_last()
                        dst = dstb[:, 2048 * pt + 512 * qt:2048 * pt + 512 * qt + 512]
                        ev_qke.append(elu_evac(bk, dst))
                if which == 0:
                    fr["w0lo"] = te_last()
                    evv = load_region(srcv, 0, 2048)
                else:
                    fr["w1lo"] = te_last()

            P.waits("dve", [fr["arena"]])
            ev_vones = P.do("dve", lambda e: e.memset(vb4[:, :, :, 64:65], 1.0))
            ev_ktv = []
            ev_vev = []
            for tc in range(NTC):
                kb, vb = B[2 + tc % 2], B[4 + tc % 2]
                for b in (kb, vb):
                    if b.free_ev is not None:
                        P.waits("te", b.free_ev if isinstance(b.free_ev, list) else [b.free_ev])
                P.waits("te", [evq, evv, ev_xn_ready[tc // 4]])
                for ec in range(NE):
                    lhsT = XN[ec][:, 128 * tc:128 * tc + 128]
                    P.do("te", lambda e, b=kb, l=lhsT, r=WREG[1][:, 256 * ec:256 * ec + 256],
                         s=(ec == 0), t=(ec == NE - 1):
                         e.matmul(b.ap()[:, 0:256], l, r, start=s, stop=t))
                kb.data_ev = te_last()
                for ec in range(NE):
                    lhsT = XN[ec][:, 128 * tc:128 * tc + 128]
                    P.do("te", lambda e, b=vb, l=lhsT, r=WREG[0][:, 256 * ec:256 * ec + 256],
                         s=(ec == 0), t=(ec == NE - 1):
                         e.matmul(b.ap()[:, 0:256], l, r, start=s, stop=t))
                vb.data_ev = te_last()
                dstk = KT[:, 256 * tc:256 * tc + 256]
                P.waits("act", [kb.data_ev, fr["gsc0"]])
                e1 = P.do("act", lambda e, b=kb: e.activation(
                    out=GSC[:, 0:256], in_=b.ap()[:, 0:256], func=AF.Exp))
                e2 = P.do("act", lambda e, b=kb: e.activation(
                    out=GSC[:, 256:512], in_=b.ap()[:, 0:256], func=AF.Relu))
                kb.free_ev = e2
                P.wait("dve", e2)
                evkt = P.do("dve", lambda e, d=dstk: e.scalar_tensor_tensor(
                    out=d, in0=GSC[:, 0:256], scalar=1.0, in1=GSC[:, 256:512],
                    op0=ALU.min, op1=ALU.add))
                fr["gsc0"] = evkt
                ev_ktv.append(evkt)
                dsel = vb4[:, :, tc, 0:64]
                srcap = vb.ap()[:, 0:256].rearrange("p (h x) -> p h x", h=4)
                P.waits("dve", [vb.data_ev, ev_vones])
                evve = P.do("dve", lambda e, d=dsel, s=srcap: e.tensor_copy(out=d, in_=s))
                vb.free_ev = evve
                ev_vev.append(evve)
            fr["w0lo"] = te_last()
            fr["w1lo"] = te_last()
            evo = load_region(srco, 1, 2048, k=2)

            P.waits("dve", [fr[f"ppb{k}"] for k in range(4)])
            for k in range(4):
                fr[f"ppb{k}"] = None
            ev_aout = []
            sold = {}
            st_ev = {}
            for h in range(4):
                r0, cb0 = 64 * (h % 2), (h // 2) * 130
                sold[h] = 0
                st_ev[h] = P.do("dve", lambda e, r0=r0, cb0=cb0: e.memset(
                    SSTT[r0:r0 + 64, cb0:cb0 + 65], 0.0))
            for quad in range(4):
                obs = {h: B[2 + h] for h in range(4)}
                for h in range(4):
                    ob = obs[h]
                    if ob.free_ev is not None:
                        P.waits("te", ob.free_ev if isinstance(ob.free_ev, list) else [ob.free_ev])
                steps = [(cc, h) for cc in range(4) for h in range(4)]
                LAGB = 3
                fdat = {}

                def front_b(cc, h, si):
                    gh = 4 * qr + h
                    pt, base = h // 2, 64 * (h % 2)
                    c = 4 * quad + cc
                    sbk = B[si % 2]
                    if sbk.free_ev is not None:
                        P.waits("te", sbk.free_ev if isinstance(sbk.free_ev, list) else [sbk.free_ev])
                    lhsT = KB[base:base + 64, 2048 * pt + 128 * c:2048 * pt + 128 * c + 128]
                    rhs = QB[base:base + 64, 2048 * pt + 128 * c:2048 * pt + 128 * c + 128]
                    ev_s = P.do("te", lambda e, b=sbk, l=lhsT, r=rhs:
                                e.matmul(b.ap()[:, 0:128], l, r, start=True, stop=True))
                    slot = si % 4
                    pslice = PPB[:, 512 * slot:512 * slot + 128]
                    P.waits("dve", [ev_s, fr[f"ppb{slot}"], ev_consts])
                    ev_pp = P.do("dve", lambda e, d=pslice, b=sbk, c=c, gh=gh:
                                 e.scalar_tensor_tensor(
                                     out=d, in0=b.ap()[:, 0:128],
                                     scalar=EXPMLS[:, 16 * c + gh:16 * c + gh + 1],
                                     in1=CONES[:], op0=ALU.mult, op1=ALU.mult))
                    sbk.free_ev = ev_pp
                    ksl = si % 4
                    P.waits("dve", [fr[f"ktil{ksl}"], ev_ktv[c], ev_explc])
                    ev_kt = P.do("dve", lambda e, sl=ksl, c=c, gh=gh, h=h: e.tensor_scalar(
                        out=KTIL[:, 64 * sl:64 * sl + 64],
                        in0=KT[:, 256 * c + 64 * h:256 * c + 64 * h + 64],
                        scalar1=EXPLCM[:, 16 * c + gh:16 * c + gh + 1], scalar2=None,
                        op0=ALU.mult))
                    fdat[(cc, h)] = (ev_pp, pslice, ev_kt, ksl, slot)

                def back_b(cc, h, si):
                    gh = 4 * qr + h
                    pt, base = h // 2, 64 * (h % 2)
                    r0, cb0 = 64 * (h % 2), (h // 2) * 130
                    c = 4 * quad + cc
                    ob = obs[h]
                    ev_pp, pslice, ev_kt, ksl, slot = fdat.pop((cc, h))
                    rhs = QB[base:base + 64, 2048 * pt + 128 * c:2048 * pt + 128 * c + 128]
                    vblk = VAb[:, 65 * (h * 16 + c):65 * (h * 16 + c) + 65]
                    P.waits("te", [ev_pp, ev_vev[c]])
                    ev_pvl = P.do("te", lambda e, b=ob, v=vblk, p=pslice, cc=cc:
                                  e.matmul(b.ap()[0:65, 128 * cc:128 * cc + 128], v, p,
                                           start=(cc == 0), stop=False))
                    fr[f"ppb{slot}"] = ev_pvl
                    P.waits("te", [st_ev[h]])
                    P.do("te", lambda e, b=ob, s=sold[h], cc=cc, r0=r0, cb0=cb0, rr=rhs:
                         e.matmul(b.ap()[0:65, 128 * cc:128 * cc + 128],
                                  SSTT[r0:r0 + 64, cb0 + 65 * s:cb0 + 65 * s + 65],
                                  rr, start=False, stop=False))
                    stb = B[6 + si % 2]
                    if stb.free_ev is not None:
                        P.waits("te", stb.free_ev if isinstance(stb.free_ev, list) else [stb.free_ev])
                    P.waits("te", [ev_kt, ev_vev[c]])
                    ev_st = P.do("te", lambda e, b=stb, sl=ksl, v=vblk, r0=r0: e.matmul(
                        b.ap()[r0:r0 + 64, 0:65], KTIL[:, 64 * sl:64 * sl + 64], v,
                        start=True, stop=True))
                    fr[f"ktil{ksl}"] = ev_st
                    so, sn = sold[h], 1 - sold[h]
                    P.waits("dve", [ev_st, st_ev[h]])
                    st_ev[h] = P.do("dve", lambda e, b=stb, so=so, sn=sn, c=c, gh=gh,
                                    r0=r0, cb0=cb0:
                                    e.scalar_tensor_tensor(
                                        out=SSTT[r0:r0 + 64, cb0 + 65 * sn:cb0 + 65 * sn + 65],
                                        in0=SSTT[r0:r0 + 64, cb0 + 65 * so:cb0 + 65 * so + 65],
                                        scalar=EXPLC[r0:r0 + 64, 16 * c + gh:16 * c + gh + 1],
                                        in1=b.ap()[r0:r0 + 64, 0:65], op0=ALU.mult, op1=ALU.add))
                    stb.free_ev = st_ev[h]
                    sold[h] = sn

                for i in range(len(steps) + LAGB):
                    if i >= LAGB:
                        back_b(*steps[i - LAGB], i - LAGB)
                    if i < len(steps):
                        front_b(*steps[i], i)
                epi = {}
                for h in range(4):
                    gh = 4 * qr + h
                    ob = obs[h]
                    P.waits("te", [ev_decay_t, ev_consts])
                    ev_eps = P.do("te", lambda e, b=ob, gh=gh, quad=quad: e.matmul(
                        b.ap()[0:65, :], EPSSEL[:, 65 * gh:65 * gh + 65],
                        EXPMLT[:, 512 * quad:512 * quad + 512], start=False, stop=True))
                    ob.data_ev = ev_eps
                for h in range(4):
                    ob = obs[h]
                    nsl = h % 2
                    P.waits("dve", [fr[f"nums{nsl}"], ob.data_ev])
                    ev_nm = P.do("dve", lambda e, b=ob, n=nsl: e.tensor_copy(
                        out=NUMS[0:64, 512 * n:512 * n + 512], in_=b.ap()[0:64, :]))
                    P.waits("act", [ob.data_ev, fr["lnd"], fr["rcb"]])
                    ev_l = P.do("act", lambda e, b=ob: e.activation(
                        out=LND[64:65, :], in_=b.ap()[64:65, :], func=AF.Ln))
                    P.wait("act", ev_l)
                    ev_rc = P.do("act", lambda e: e.activation(
                        out=RCB[64:65, :], in_=LND[64:65, :], func=AF.Exp, scale=-1.0))
                    fr["lnd"] = ev_rc
                    epi[h] = (ev_nm, ev_rc, nsl)
                    gh = 4 * qr + h
                    pt = h // 2
                    ev_nm, ev_rc, nsl = epi[h]
                    rbk = B[6]
                    if rbk.free_ev is not None:
                        P.waits("te", rbk.free_ev if isinstance(rbk.free_ev, list) else [rbk.free_ev])
                    P.waits("te", [ev_rc])
                    ev_rb = P.do("te", lambda e, b=rbk: e.matmul(
                        b.ap()[0:64, :], ONESB[64:65, 0:64], RCB[64:65, :], start=True, stop=True))
                    fr["rcb"] = ev_rb
                    if h % 2 == 0:
                        dst = AOUTb[0:64, 2048 * pt + 512 * quad:2048 * pt + 512 * quad + 512]
                        P.waits("dve", [ev_nm, ev_rb])
                        ev_mul = P.do("dve", lambda e, d=dst, b=rbk, n=nsl: e.tensor_tensor(
                            out=d, in0=NUMS[0:64, 512 * n:512 * n + 512],
                            in1=b.ap()[0:64, :], op=ALU.mult))
                        rbk.free_ev = ev_mul
                        ob.free_ev = ev_mul
                        fr[f"nums{nsl}"] = ev_mul
                        ev_aout.append(ev_mul)
                    else:
                        P.waits("dve", [ev_nm, ev_rb, fr["attnh"]])
                        ev_mul = P.do("dve", lambda e, b=rbk, n=nsl: e.tensor_tensor(
                            out=ATTNH[0:64, :], in0=NUMS[0:64, 512 * n:512 * n + 512],
                            in1=b.ap()[0:64, :], op=ALU.mult))
                        rbk.free_ev = ev_mul
                        ob.free_ev = ev_mul
                        fr[f"nums{nsl}"] = ev_mul
                        mbk = B[7]
                        if mbk.free_ev is not None:
                            P.waits("te", mbk.free_ev if isinstance(mbk.free_ev, list) else [mbk.free_ev])
                        P.waits("te", [ev_mul, ev_consts])
                        ev_mv = P.do("te", lambda e, b=mbk: e.matmul(
                            b.ap()[:, :], IDSH[:], ATTNH[0:64, :], start=True, stop=True))
                        fr["attnh"] = ev_mv
                        dst = AOUTb[64:128, 2048 * pt + 512 * quad:2048 * pt + 512 * quad + 512]
                        P.wait("dve", ev_mv)
                        ev_mc = P.do("dve", lambda e, d=dst, b=mbk: e.tensor_copy(
                            out=d, in_=b.ap()[64:128, :]))
                        mbk.free_ev = ev_mc
                        ev_aout.append(ev_mc)
            for fc in range(NE):
                for qt in range(NQT):
                    bk = B[6 + (fc * 4 + qt) % 2]
                    if bk.free_ev is not None:
                        P.waits("te", bk.free_ev if isinstance(bk.free_ev, list) else [bk.free_ev])
                    P.waits("te", [evo] + ev_aout)
                    for pt in range(2):
                        lhsT = WREG[1][:, 1024 * pt + 128 * fc:1024 * pt + 128 * fc + 128]
                        rhs = AOUTb[:, 2048 * pt + 512 * qt:2048 * pt + 512 * qt + 512]
                        P.do("te", lambda e, b=bk, l=lhsT, r=rhs, s=(pt == 0), t=(pt == 1):
                             e.matmul(b.ap()[:, :], l, r, start=s, stop=t))
                    bk.data_ev = te_last()
                    P.wait("dve", bk.data_ev)
                    ev = P.do("dve", lambda e, fc=fc, qt=qt, b=bk: e.scalar_tensor_tensor(
                        out=RS[fc][:, 512 * qt:512 * qt + 512], in0=b.ap()[:, :], scalar=1.0,
                        in1=RS[fc][:, 512 * qt:512 * qt + 512], op0=ALU.mult, op1=ALU.add))
                    bk.free_ev = ev
                    rs_ev[(fc, qt)] = ev
                    ev_rs_last = ev
            fr["arena"] = ev_rs_last
            fr["w1lo"] = te_last()
        xn_guard[0] = te_last()
        return ev_rs_last

    ev_xn_ready = None
    for ph in (phase_list if phase_list is not None else PHASES[:upto]):
        if ph.startswith("ln"):
            ev_xn_ready = emit_ln()
        elif ph == "attn_a":
            emit_attn_a()
        elif ph in ("ffn1", "ffn2"):
            emit_ffn()
        elif ph == "attn_b":
            emit_attn_b()

    if dump_xn:
        fin = P.barrier()
        P.waits("sy", fin)
        for c in range(NE):
            P.do("gp", lambda e, c=c: e.dma_start(out=ydbg[128 * c:128 * (c + 1), :],
                                                  in_=XN[c][:]), sem="dy", amt=16)
    for c in range(NE):
        P.waits("sy", [rs_ev.get((c, qt)) for qt in range(NQT)])
        P.do("sy", lambda e, c=c: e.dma_start(out=y[128 * c:128 * (c + 1), :], in_=RS[c][:]),
             sem="dy", amt=16)
    P.run("sy", lambda e: e.wait_ge(semh["dy"], P.cnt["dy"]))

    with nc.Block() as blk:
        S = dict(semh)
        for name, deco in (("sy", blk.sync), ("gp", blk.gpsimd), ("te", blk.tensor),
                           ("act", blk.scalar), ("dve", blk.vector)):
            def mk(fns):
                def _(e):
                    for f in fns:
                        f(e, S)
                return _
            deco(mk(P.ops[name]))
    ctx.close()
    return nc


def prep_inputs(inputs):
    x = np.asarray(inputs["L0_attn_a_query"], dtype=np.float32)
    w_ip = np.ascontiguousarray(np.asarray(inputs["L0_attn_a_in_proj_weight"], np.float32).T)
    w_op = np.ascontiguousarray(np.asarray(inputs["L0_attn_a_out_proj_weight"], np.float32).T)
    wq = np.ascontiguousarray(np.asarray(inputs["L0_attn_b_Wq"], np.float32))
    wk = np.ascontiguousarray(np.asarray(inputs["L0_attn_b_Wk"], np.float32))
    wv = np.ascontiguousarray(np.asarray(inputs["L0_attn_b_Wv"], np.float32))
    wo = np.ascontiguousarray(np.asarray(inputs["L0_attn_b_Wo"], np.float32))
    wa = np.ascontiguousarray(np.asarray(inputs["L0_attn_b_Wa"], np.float32))
    ba = np.ascontiguousarray(np.asarray(inputs["L0_attn_b_ba"], np.float32).reshape(16, 1))
    wg = np.asarray(inputs["L0_ffn_w_gate"])
    wu = np.asarray(inputs["L0_ffn_w_up"])
    wd = np.asarray(inputs["L0_ffn_w_down"])

    def tile_gu(w):
        t = w.reshape(NHC, 128, NE, 128).transpose(3, 0, 2, 1)
        return np.ascontiguousarray(t.reshape(128, NHC * NE * 128).astype(np.int8))

    def tile_d(w):
        t = w.reshape(NE, 128, NHC, 128).transpose(3, 0, 2, 1)
        return np.ascontiguousarray(t.reshape(128, NE * NHC * 128).astype(np.int8))

    cones, idsh, idf32, epssel = host_consts()
    shared = {"w_ip": w_ip, "w_op": w_op, "wq": wq, "wk": wk, "wv": wv, "wo": wo,
              "wa": wa, "ba": ba, "wg8": tile_gu(wg), "wu8": tile_gu(wu), "wd8": tile_d(wd),
              "c_cones": cones, "c_idsh": idsh, "c_idf": idf32, "c_eps": epssel}
    in_maps = []
    for b in range(8):
        m = dict(shared)
        m["xt"] = np.ascontiguousarray(x[b].T)
        in_maps.append(m)
    return in_maps


def kernel(**inputs):
    in_maps = prep_inputs(inputs)
    nc = build()
    res = run_bass_kernel_spmd(nc, in_maps, core_ids=list(range(8)))
    out = np.empty((8, T, D), dtype=np.float32)
    for b in range(8):
        out[b] = res.results[b]["y"].T
    return out
